# version 02 (final)
# baseline (speedup 1.0000x reference)
import contextlib
import numpy as np
import ml_dtypes
import concourse.bass as bass
import concourse.mybir as mybir

F32 = mybir.dt.float32
BF16 = mybir.dt.bfloat16
AX = mybir.AxisListType.X
ACT = mybir.ActivationFunctionType
ALU = mybir.AluOpType
BF = ml_dtypes.bfloat16

D = 1024
H = 16
DH = 64
C = 128
LN_EPS = 1e-5
RMS_EPS = float(np.finfo(np.float32).eps)
ENG = {"t": "tensor", "v": "vector", "s": "scalar", "g": "gpsimd", "i": "sync"}


def ap3(base, dims):
    return bass.AP(base.tensor, base.offset, [list(base.ap[0])] + dims)


class Ctx:
    def __init__(self, nc, block, stack, limit=10**9):
        self.nc = nc
        self.block = block
        self.c = {k: 0 for k in ENG}
        self.sem = {k: stack.enter_context(nc.semaphore("sem_" + k)) for k in ENG}
        self.limit = limit
        self.nstep = 0

    def step(self, eng, deps, n_inc=1):
        self.nstep += 1
        if self.nstep > self.limit:
            return lambda f: None
        sem = self.sem[eng]
        mydeps = [(self.sem[k], v) for k, v in deps.items() if v > 0 and k != eng]
        if self.c[eng] > 0:
            mydeps.append((sem, self.c[eng]))
        reg = getattr(self.block, ENG[eng])

        def deco(f):
            def wrapped(e):
                for s_, v_ in mydeps:
                    e.wait_ge(s_, v_)
                last = f(e)
                last.then_inc(sem, n_inc)
            reg(wrapped)
            self.c[eng] += n_inc
            return None
        return deco

    def dma(self, eng, deps):
        self.nstep += 1
        if self.nstep > self.limit:
            return lambda f: None
        sem = self.sem[eng]
        mydeps = [(self.sem[k], v) for k, v in deps.items() if v > 0 and k != eng]
        reg = getattr(self.block, ENG[eng])
        holder = {}

        def deco(f):
            def wrapped(e):
                for s_, v_ in mydeps:
                    e.wait_ge(s_, v_)
                insts = f(e)
                for i_ in insts:
                    i_.then_inc(sem, 16)
                holder["n"] = len(insts)
            reg(wrapped)
            self.c[eng] += 16 * holder["n"]
            return None
        return deco

    def wait(self, eng, deps):
        reg = getattr(self.block, ENG[eng])
        mydeps = [(self.sem[k], v) for k, v in deps.items() if v > 0]

        def f(e):
            for s_, v_ in mydeps:
                e.wait_ge(s_, v_)
        reg(f)

    def bar(self):
        return dict(self.c)


def inline_consts(nc):
    t = np.arange(C, dtype=np.float32)
    return {
        "DLC": nc.inline_tensor(np.maximum(t[:, None] - t[None, :], 0.0).astype(np.float32), name="c_dlc").ap(),
        "LMASK": nc.inline_tensor((t[:, None] > t[None, :]).astype(np.float32), name="c_lmask").ap(),
        "UDLC": nc.inline_tensor(np.maximum(t[None, :] - t[:, None], 0.0).astype(np.float32), name="c_udlc").ap(),
        "UMASK": nc.inline_tensor((t[None, :] > t[:, None]).astype(np.float32), name="c_umask").ap(),
        "IDENT": nc.inline_tensor(np.eye(128, dtype=np.float32).astype(BF), name="c_ident").ap(),
        "TP1": nc.inline_tensor((t + 1.0).reshape(C, 1).astype(np.float32), name="c_tp1").ap(),
        "CKC": nc.inline_tensor((C - 1.0 - t).reshape(C, 1).astype(np.float32), name="c_ckc").ap(),
        "ONESROW": nc.inline_tensor(np.ones((1, 128), np.float32), name="c_ones").ap(),
    }


def alloc_attn(nc, stack, B_LOC):
    b = {}

    def S(name, shape, dt=F32):
        b[name] = stack.enter_context(nc.sbuf_tensor("a_" + name, shape, dt))

    S("WT", [128, 8 * D], BF16)
    S("Mmask", [128, H * C]); S("Nmask", [128, H * C])
    S("W_REP", [128, D]); S("ONW_REP", [128, D]); S("ONB_REP", [128, D])
    S("decayK", [128, D]); S("lamCexp", [64, B_LOC * H * DH])
    S("gTAB", [128, H]); S("BCOL", [128, H]); S("LNLAM", [128, H])
    S("DKcols", [128, H]); S("W2COL", [128, 1]); S("LAMCT", [128, H])
    S("S_all", [64, B_LOC * H * DH]); S("Sbf", [64, 64 + B_LOC * H * DH], BF16)
    S("IDENT", [128, 128], BF16)
    S("cDLC", [128, 128]); S("cLM", [128, 128]); S("cUDLC", [128, 128])
    S("cUM", [128, 128]); S("cTP1", [128, 1]); S("cCKC", [128, 1])
    S("cONES", [1, 128])
    S("knw_r", [1, 64]); S("lamlog_r", [1, 16]); S("eta_r", [1, 16])
    S("onw_r", [1, D]); S("onb_r", [1, D])
    S("lam_r", [1, 16]); S("lnlam_r", [1, 16]); S("etat_r", [1, 16])
    S("lamC_r", [1, 16]); S("beta_r", [1, 16]); S("wexp_r", [1, D])
    S("SCRA", [128, 128]); S("SCRB", [128, 128]); S("SCRC1", [128, 1])
    S("stdcol", [128, 1]); S("srt16", [128, H]); S("epsc", [128, 2]); S("ONESBIG", [128, 2048])
    S("Wtmp", [128, D]); S("Wbf", [128, D], BF16)
    S("chC0", [128, H * C], BF16); S("chC1", [128, H * C], BF16)
    S("chD0", [128, H * C], BF16); S("chD1", [128, H * C], BF16)
    S("chrC", [128, H * C], BF16); S("chrD", [128, H * C], BF16)
    S("xt0", [128, D]); S("xt1", [128, D])
    S("xhat", [128, D]); S("xhatbf", [128, D], BF16)
    S("xF", [128, D], BF16); S("kF2", [64, H * 128], BF16)
    S("krawbf", [128, D], BF16); S("ktprep", [128, D], BF16)
    S("EXP1", [128, D]); S("irm_exp", [128, D]); S("irm_cexp", [128, H * C])
    S("static", [128, D]); S("P1", [128, D]); S("Rp", [128, D])
    S("X", [128, D]); S("Xbf", [128, D], BF16); S("OD", [128, D])
    S("TMP", [128, 512])
    S("y10", [128, D]); S("y11", [128, D])
    S("negmu", [128, 1]); S("ssq", [128, 1]); S("var", [128, 1]); S("invstd", [128, 1])
    S("sumsq_x", [128, H]); S("wsumsq", [128, H]); S("invrms", [128, H])
    S("bsc", [128, H]); S("grT", [128, H]); S("gr2T", [128, H])
    b["xt"] = [b["xt0"], b["xt1"]]
    b["y1b"] = [b["y10"], b["y11"]]
    b["chC"] = [b["chC0"], b["chC1"]]
    b["chD"] = [b["chD0"], b["chD1"]]
    b["pb"] = [stack.enter_context(nc.psum_tensor("pb%d" % i, [128, 512], F32))
               for i in range(6)]
    b["pt"] = stack.enter_context(nc.psum_tensor("pt", [128, 1024], BF16))
    b["pt2"] = stack.enter_context(nc.psum_tensor("pt2", [128, 1024], BF16))
    return b


def head_kf(b, h):
    return b["kF2"][0:64, h * 128:(h + 1) * 128]


def scol(bb, h):
    return 0, (bb * H + h) * DH


def emit_attn_setup(ctx, b, io, cst, B_LOC):
    pb = b["pb"]

    @ctx.dma("i", {})
    def _(sy):
        out = []
        out.append(sy.dma_start(out=b["IDENT"][:], in_=cst["IDENT"]))
        out.append(sy.dma_start(out=b["cDLC"][:], in_=cst["DLC"]))
        out.append(sy.dma_start(out=b["cLM"][:], in_=cst["LMASK"]))
        out.append(sy.dma_start(out=b["cUDLC"][:], in_=cst["UDLC"]))
        out.append(sy.dma_start(out=b["cUM"][:], in_=cst["UMASK"]))
        out.append(sy.dma_start(out=b["cTP1"][:], in_=cst["TP1"]))
        out.append(sy.dma_start(out=b["cCKC"][:], in_=cst["CKC"]))
        out.append(sy.dma_start(out=b["cONES"][:], in_=cst["ONESROW"]))
        out.append(sy.dma_start(out=b["knw_r"][:], in_=io["knw"]))
        out.append(sy.dma_start(out=b["lamlog_r"][:], in_=io["lamlog"]))
        out.append(sy.dma_start(out=b["eta_r"][:], in_=io["eta"]))
        out.append(sy.dma_start(out=b["onw_r"][:], in_=io["onw"]))
        out.append(sy.dma_start(out=b["onb_r"][:], in_=io["onb"]))
        out.append(sy.dma_start(out=b["Wtmp"][:], in_=io["W"][0:128, :]))
        out.append(sy.dma_start(out=b["W2COL"][0:64, 0:1], in_=io["knw"]))
        out.append(sy.dma_start(out=b["W2COL"][64:128, 0:1], in_=io["knw"]))
        return out

    @ctx.step("g", {})
    def _(g):
        g.memset(b["ONESBIG"][:], 1.0)
        g.memset(b["epsc"][:, 0:1], LN_EPS)
        return g.memset(b["epsc"][:, 1:2], RMS_EPS)

    @ctx.step("s", ctx.bar())
    def _(s):
        s.activation(b["lam_r"][:], b["lamlog_r"][:], ACT.Sigmoid)
        s.activation(b["lnlam_r"][:], b["lam_r"][:], ACT.Ln)
        s.activation(b["etat_r"][:], b["eta_r"][:], ACT.Tanh)
        return s.activation(b["lamC_r"][:], b["lnlam_r"][:], ACT.Exp, scale=float(C))

    @ctx.step("v", ctx.bar())
    def _(v):
        return v.tensor_scalar(b["beta_r"][:], b["lam_r"][:], -1.0, 1.0 / 64.0, ALU.add, ALU.mult)

    @ctx.step("v", ctx.bar())
    def _(v):
        v.tensor_tensor(b["beta_r"][:], b["beta_r"][:], b["etat_r"][:], ALU.mult)
        last = None
        for hh in range(16):
            last = v.tensor_copy(b["wexp_r"][:1, hh * 64:(hh + 1) * 64], b["knw_r"][:1, :64])
        return last

    @ctx.step("t", ctx.bar())
    def _(t):
        t.matmul(pb[0][:, 0:16], b["cONES"][:], b["lnlam_r"][:], start=True, stop=True)
        t.matmul(pb[0][:, 16:32], b["cONES"][:], b["beta_r"][:], start=True, stop=True)
        t.matmul(pb[0][:, 32:48], b["cONES"][:], b["lamC_r"][:], start=True, stop=True)
        t.matmul(pb[1][:, :], b["cONES"][:], b["wexp_r"][:1, 0:512], start=True, stop=True)
        t.matmul(pb[2][:, :], b["cONES"][:], b["wexp_r"][:1, 512:1024], start=True, stop=True)
        t.matmul(pb[3][:, :], b["cONES"][:], b["onw_r"][:1, 0:512], start=True, stop=True)
        return t.matmul(pb[4][:, :], b["cONES"][:], b["onw_r"][:1, 512:1024], start=True, stop=True)

    @ctx.step("v", ctx.bar())
    def _(v):
        v.tensor_copy(b["LNLAM"][:], pb[0][:, 0:16])
        v.tensor_copy(b["BCOL"][:], pb[0][:, 16:32])
        v.tensor_copy(b["LAMCT"][:], pb[0][:, 32:48])
        v.tensor_copy(b["W_REP"][:, 0:512], pb[1][:])
        v.tensor_copy(b["W_REP"][:, 512:1024], pb[2][:])
        v.tensor_copy(b["ONW_REP"][:, 0:512], pb[3][:])
        return v.tensor_copy(b["ONW_REP"][:, 512:1024], pb[4][:])

    @ctx.step("t", ctx.bar())
    def _(t):
        t.matmul(pb[1][:, :], b["cONES"][:], b["onb_r"][:1, 0:512], start=True, stop=True)
        return t.matmul(pb[2][:, :], b["cONES"][:], b["onb_r"][:1, 512:1024], start=True, stop=True)

    @ctx.step("v", ctx.bar())
    def _(v):
        v.tensor_copy(b["ONB_REP"][:, 0:512], pb[1][:])
        return v.tensor_copy(b["ONB_REP"][:, 512:1024], pb[2][:])

    @ctx.step("v", ctx.bar())
    def _(v):
        last = None
        for bb in range(B_LOC):
            for h in range(H):
                r0, c0 = scol(bb, h)
                last = v.tensor_scalar(b["lamCexp"][0:64, c0:c0 + 64],
                                       b["ONESBIG"][0:64, 0:64],
                                       b["LAMCT"][0:64, h:h + 1], None, ALU.mult)
        return last

    for h in range(H):
        @ctx.step("s", ctx.bar())
        def _(s, h=h):
            s.activation(b["SCRA"][:], b["cDLC"][:], ACT.Exp, scale=b["LNLAM"][:, h:h + 1])
            s.activation(b["SCRB"][:], b["cUDLC"][:], ACT.Exp, scale=b["LNLAM"][:, h:h + 1])
            s.activation(b["gTAB"][:, h:h + 1], b["cTP1"][:], ACT.Exp, scale=b["LNLAM"][:, h:h + 1])
            return s.activation(b["SCRC1"][:], b["cCKC"][:], ACT.Exp, scale=b["LNLAM"][:, h:h + 1])

        @ctx.step("v", ctx.bar())
        def _(v, h=h):
            v.scalar_tensor_tensor(b["Mmask"][:, h * C:(h + 1) * C], b["SCRA"][:],
                                   b["BCOL"][:, h:h + 1], b["cLM"][:], ALU.mult, ALU.mult)
            v.scalar_tensor_tensor(b["Nmask"][:, h * C:(h + 1) * C], b["SCRB"][:],
                                   b["BCOL"][:, h:h + 1], b["cUM"][:], ALU.mult, ALU.mult)
            return v.tensor_scalar(b["DKcols"][:, h:h + 1], b["SCRC1"][:],
                                   b["BCOL"][:, h:h + 1], None, ALU.mult)

    @ctx.step("v", ctx.bar())
    def _(v):
        last = None
        for h in range(H):
            last = v.tensor_scalar(b["decayK"][:, h * 64:(h + 1) * 64],
                                   b["ONESBIG"][:, 0:64],
                                   b["DKcols"][:, h:h + 1], None, ALU.mult)
        return last

    for j in range(8):
        if j > 0:
            @ctx.dma("i", ctx.bar())
            def _(sy, j=j):
                return [sy.dma_start(out=b["Wtmp"][:], in_=io["W"][128 * j:128 * (j + 1), :])]

        @ctx.step("s", ctx.bar())
        def _(s):
            return s.copy(b["Wbf"][:], b["Wtmp"][:])

        @ctx.step("t", ctx.bar())
        def _(t):
            last = None
            for i in range(8):
                last = t.transpose(b["pt"][:, i * 128:(i + 1) * 128],
                                   b["Wbf"][:, i * 128:(i + 1) * 128], b["IDENT"][:])
            return last

        @ctx.step("s", ctx.bar())
        def _(s, j=j):
            last = None
            for i in range(8):
                last = s.copy(b["WT"][:, i * D + j * 128:i * D + (j + 1) * 128],
                              b["pt"][:, i * 128:(i + 1) * 128])
            return last

    @ctx.step("g", ctx.bar())
    def _(g):
        g.memset(b["S_all"][:], 0.0)
        return g.memset(b["Sbf"][:], 0.0)


def emit_attn_tile(ctx, b, io, bb, cc, ti, B_LOC, dbg):
    pb = b["pb"]
    xt = b["xt"][ti % 2]
    y1 = b["y1b"][ti % 2]
    need_in = ctx.c["i"]

    @ctx.step("v", dict(ctx.bar(), i=need_in))
    def _(v):
        return v.tensor_reduce(b["SCRC1"][:], xt[:], AX, ALU.add)

    @ctx.step("v", ctx.bar())
    def _(v):
        return v.tensor_scalar(b["negmu"][:], b["SCRC1"][:], -1.0 / D, None, ALU.mult)

    @ctx.step("s", dict(ctx.bar(), i=need_in))
    def _(s):
        return s.activation(b["EXP1"][:], xt[:], ACT.Square)

    @ctx.step("v", ctx.bar())
    def _(v):
        return v.tensor_reduce(b["ssq"][:], b["EXP1"][:], AX, ALU.add)

    @ctx.step("v", ctx.bar())
    def _(v):
        v.tensor_scalar(b["var"][:], b["ssq"][:], 1.0 / D, None, ALU.mult)
        return v.tensor_tensor(b["SCRC1"][:], b["negmu"][:], b["negmu"][:], ALU.mult)

    @ctx.step("v", ctx.bar())
    def _(v):
        return v.tensor_tensor(b["var"][:], b["var"][:], b["SCRC1"][:], ALU.subtract)

    @ctx.step("s", ctx.bar())
    def _(s):
        return s.activation(b["stdcol"][:], b["var"][:], ACT.Sqrt, bias=b["epsc"][:, 0:1])

    @ctx.step("v", ctx.bar())
    def _(v):
        return v.reciprocal(b["invstd"][:], b["stdcol"][:])

    @ctx.step("v", ctx.bar())
    def _(v):
        return v.tensor_scalar(b["xhat"][:], xt[:], b["negmu"][:], b["invstd"][:],
                               ALU.add, ALU.mult)

    @ctx.step("s", ctx.bar())
    def _(s):
        return s.copy(b["xhatbf"][:], b["xhat"][:])

    @ctx.step("t", ctx.bar())
    def _(t):
        last = None
        for i in range(8):
            last = t.transpose(b["pt"][:, i * 128:(i + 1) * 128],
                               b["xhatbf"][:, i * 128:(i + 1) * 128], b["IDENT"][:])
        for h in range(8):
            last = t.transpose(b["pt2"][0:64, h * 128:(h + 1) * 128],
                               b["xhatbf"][:, h * 64:(h + 1) * 64], b["IDENT"][:])
        return last

    @ctx.step("s", ctx.bar())
    def _(s):
        last = None
        for i in range(8):
            last = s.copy(b["xF"][:, i * 128:(i + 1) * 128],
                          b["pt"][:, i * 128:(i + 1) * 128])
        return last

    @ctx.step("v", ctx.bar())
    def _(v):
        return v.tensor_scalar(b["kF2"][0:64, 0:1024], b["pt2"][0:64, 0:1024],
                               b["W2COL"][0:64, 0:1], None, ALU.mult)

    @ctx.step("t", ctx.bar())
    def _(t):
        last = None
        for h in range(8):
            last = t.transpose(b["pt2"][0:64, h * 128:(h + 1) * 128],
                               b["xhatbf"][:, (8 + h) * 64:(9 + h) * 64], b["IDENT"][:])
        return last

    @ctx.step("v", ctx.bar())
    def _(v):
        v.tensor_scalar(b["kF2"][0:64, 1024:2048], b["pt2"][0:64, 0:1024],
                        b["W2COL"][0:64, 0:1], None, ALU.mult)
        v.tensor_tensor(b["krawbf"][:], b["xhat"][:], b["W_REP"][:], ALU.mult)
        return v.tensor_reduce(b["sumsq_x"][:], ap3(b["EXP1"][:], [[64, 16], [1, 64]]), AX, ALU.add)

    @ctx.step("v", ctx.bar())
    def _(v):
        return v.tensor_scalar(b["sumsq_x"][:], b["sumsq_x"][:], 1.0 / DH, None, ALU.mult)

    @ctx.step("s", ctx.bar())
    def _(s):
        s.activation(b["EXP1"][:], b["krawbf"][:], ACT.Square)
        return s.activation(b["srt16"][:], b["sumsq_x"][:], ACT.Sqrt, bias=b["epsc"][:, 1:2])

    @ctx.step("v", ctx.bar())
    def _(v):
        v.reciprocal(b["invrms"][:], b["srt16"][:])
        return v.tensor_reduce(b["wsumsq"][:], ap3(b["EXP1"][:], [[64, 16], [1, 64]]), AX, ALU.add)

    @ctx.step("v", ctx.bar())
    def _(v):
        v.tensor_tensor(b["ktprep"][:], b["krawbf"][:], b["decayK"][:], ALU.mult)
        src_ = b["invrms"][:]
        for h in range(H):
            v.tensor_scalar(b["irm_exp"][:, h * 64:(h + 1) * 64], b["ONESBIG"][:, 0:64],
                            b["invrms"][:, h:h + 1], None, ALU.mult)
            v.tensor_scalar(b["irm_cexp"][:, h * C:(h + 1) * C], b["ONESBIG"][:, 0:C],
                            b["invrms"][:, h:h + 1], None, ALU.mult)
        v.tensor_tensor(b["bsc"][:], b["invrms"][:], b["wsumsq"][:], ALU.mult)
        return v.tensor_tensor(b["grT"][:], b["gTAB"][:], b["invrms"][:], ALU.mult)

    @ctx.step("v", ctx.bar())
    def _(v):
        v.tensor_tensor(b["bsc"][:], b["bsc"][:], b["BCOL"][:], ALU.mult)
        return v.tensor_tensor(b["gr2T"][:], b["grT"][:], b["invrms"][:], ALU.mult)

    @ctx.step("t", ctx.bar())
    def _(t):
        last = None
        for half in range(2):
            for j in range(8):
                last = t.matmul(pb[half][:, :],
                                b["xF"][:, j * 128:(j + 1) * 128],
                                b["WT"][:, j * D + half * 512:j * D + half * 512 + 512],
                                start=(j == 0), stop=(j == 7))
        return last

    @ctx.step("s", ctx.bar())
    def _(s):
        s.copy(b["static"][:, 0:512], pb[0][:])
        return s.copy(b["static"][:, 512:1024], pb[1][:])

    @ctx.step("v", ctx.bar())
    def _(v):
        v.tensor_tensor(b["P1"][:, 0:512], pb[0][:], b["irm_exp"][:, 0:512], ALU.mult)
        return v.tensor_tensor(b["P1"][:, 512:1024], pb[1][:], b["irm_exp"][:, 512:1024], ALU.mult)

    @ctx.step("t", ctx.bar())
    def _(t):
        last = None
        for h in range(H):
            kf = head_kf(b, h)
            last = t.matmul(pb[2 + h // 4][:, (h % 4) * 128:(h % 4) * 128 + 128],
                            kf, kf, start=True, stop=True)
        for h in range(H):
            kf = head_kf(b, h)
            r0, c0 = scol(bb, h)
            last = t.matmul(pb[h // 8][:, (h % 8) * 64:(h % 8) * 64 + 64],
                            kf, b["Sbf"][0:64, 64 + c0:64 + c0 + 64], start=True, stop=True)
        return last

    @ctx.step("v", ctx.bar())
    def _(v):
        for k in range(4):
            v.tensor_tensor(b["chD"][0][:, k * 512:(k + 1) * 512], pb[2 + k][:],
                            b["Mmask"][:, k * 512:(k + 1) * 512], ALU.mult)
            v.tensor_tensor(b["chC"][0][:, k * 512:(k + 1) * 512], pb[2 + k][:],
                            b["Nmask"][:, k * 512:(k + 1) * 512], ALU.mult)
        v.tensor_tensor(b["chrC"][:], b["chC"][0][:], b["irm_cexp"][:], ALU.mult)
        return v.tensor_tensor(b["chrD"][:], b["chD"][0][:], b["irm_cexp"][:], ALU.mult)

    @ctx.step("v", ctx.bar())
    def _(v):
        last = None
        for h in range(H):
            ks = pb[h // 8][:, (h % 8) * 64:(h % 8) * 64 + 64]
            v.scalar_tensor_tensor(b["Rp"][:, h * 64:(h + 1) * 64], ks,
                                   b["gr2T"][:, h:h + 1], b["P1"][:, h * 64:(h + 1) * 64],
                                   ALU.mult, ALU.add)
            last = v.tensor_scalar(b["OD"][:, h * 64:(h + 1) * 64], ks,
                                   b["grT"][:, h:h + 1], None, ALU.mult)
        return last

    @ctx.step("s", ctx.bar())
    def _(s):
        s.copy(b["X"][:], b["Rp"][:])
        return s.copy(b["Xbf"][:], b["Rp"][:])

    def apply_level(sign, cur):
        @ctx.step("t", ctx.bar())
        def _(t):
            last = None
            for h in range(H):
                last = t.matmul(pb[h // 8][:, (h % 8) * 64:(h % 8) * 64 + 64],
                                b["chC"][cur][:, h * C:(h + 1) * C],
                                b["Xbf"][:, h * 64:(h + 1) * 64], start=True, stop=True)
            return last

        op = ALU.add if sign > 0 else ALU.subtract

        @ctx.step("v", ctx.bar())
        def _(v):
            for half in range(2):
                sl = slice(half * 512, half * 512 + 512)
                v.tensor_tensor(b["TMP"][:], pb[half][:], b["irm_exp"][:, sl], ALU.mult)
                v.tensor_tensor(b["X"][:, sl], b["X"][:, sl], b["TMP"][:], op)
            return v.tensor_copy(b["Xbf"][:], b["X"][:])

    apply_level(-1, 0)
    cur = 0
    for lvl in range(6):
        nxt = 1 - cur
        last_lvl = (lvl == 5)

        @ctx.step("t", ctx.bar())
        def _(t, cur=cur):
            last = None
            for h in range(H):
                last = t.matmul(pb[2 + h // 4][:, (h % 4) * 128:(h % 4) * 128 + 128],
                                b["chrD"][:, h * C:(h + 1) * C],
                                b["chC"][cur][:, h * C:(h + 1) * C], start=True, stop=True)
            return last

        @ctx.step("v", ctx.bar())
        def _(v, nxt=nxt):
            last = None
            for k in range(4):
                last = v.tensor_copy(b["chC"][nxt][:, k * 512:(k + 1) * 512], pb[2 + k][:])
            return last

        if not last_lvl:
            @ctx.step("t", ctx.bar())
            def _(t, cur=cur):
                last = None
                for h in range(H):
                    last = t.matmul(pb[2 + h // 4][:, (h % 4) * 128:(h % 4) * 128 + 128],
                                    b["chrC"][:, h * C:(h + 1) * C],
                                    b["chD"][cur][:, h * C:(h + 1) * C], start=True, stop=True)
                return last

            @ctx.step("v", ctx.bar())
            def _(v, nxt=nxt):
                for k in range(4):
                    v.tensor_copy(b["chD"][nxt][:, k * 512:(k + 1) * 512], pb[2 + k][:])
                v.tensor_tensor(b["chrC"][:], b["chC"][nxt][:], b["irm_cexp"][:], ALU.mult)
                return v.tensor_tensor(b["chrD"][:], b["chD"][nxt][:], b["irm_cexp"][:], ALU.mult)

        apply_level(+1, nxt)
        cur = nxt

    @ctx.step("t", ctx.bar())
    def _(t):
        last = None
        for h in range(H):
            last = t.matmul(pb[h // 8][0:64, (h % 8) * 64:(h % 8) * 64 + 64],
                            b["ktprep"][:, h * 64:(h + 1) * 64],
                            b["Xbf"][:, h * 64:(h + 1) * 64], start=True, stop=True)
        return last

    @ctx.step("v", ctx.bar())
    def _(v):
        c0 = bb * H * DH
        last = None
        for k in range(2):
            sl = slice(c0 + k * 512, c0 + k * 512 + 512)
            v.tensor_tensor(b["S_all"][0:64, sl], b["S_all"][0:64, sl],
                            b["lamCexp"][0:64, sl], ALU.mult)
            last = v.tensor_tensor(b["S_all"][0:64, sl], b["S_all"][0:64, sl],
                                   pb[k][0:64, :], ALU.subtract)
        return last

    @ctx.step("s", ctx.bar())
    def _(s):
        c0 = bb * H * DH
        return s.copy(b["Sbf"][0:64, 64 + c0:64 + c0 + 1024], b["S_all"][0:64, c0:c0 + 1024])

    @ctx.step("v", ctx.bar())
    def _(v):
        for h in range(H):
            v.tensor_scalar(b["P1"][:, h * 64:(h + 1) * 64], b["X"][:, h * 64:(h + 1) * 64],
                            b["bsc"][:, h:h + 1], None, ALU.mult)
        v.tensor_tensor(b["OD"][:], b["OD"][:], b["Rp"][:], ALU.subtract)
        v.tensor_tensor(b["OD"][:], b["OD"][:], b["X"][:], ALU.add)
        v.tensor_tensor(b["OD"][:], b["OD"][:], b["P1"][:], ALU.subtract)
        return v.tensor_tensor(b["OD"][:], b["OD"][:], b["static"][:], ALU.add)

    if dbg is not None and ti in dbg["tiles"]:
        tag = "t%d" % ti
        @ctx.dma("g", ctx.bar())
        def _(g):
            return [g.dma_start(out=dbg["xhat_" + tag], in_=b["xhat"][:]),
                    g.dma_start(out=dbg["static_" + tag], in_=b["static"][:]),
                    g.dma_start(out=dbg["NR_" + tag], in_=b["Rp"][:]),
                    g.dma_start(out=dbg["NX_" + tag], in_=b["X"][:]),
                    g.dma_start(out=dbg["OD_" + tag], in_=b["OD"][:]),
                    g.dma_start(out=dbg["S_" + tag], in_=b["S_all"][0:64, :]),
                    g.dma_start(out=dbg["EXP1_" + tag], in_=b["EXP1"][:]),
                    g.dma_start(out=dbg["cols_" + tag][:, 0:1], in_=b["negmu"][:], allow_slow_non_contiguous=True),
                    g.dma_start(out=dbg["cols_" + tag][:, 1:2], in_=b["ssq"][:], allow_slow_non_contiguous=True),
                    g.dma_start(out=dbg["cols_" + tag][:, 2:3], in_=b["var"][:], allow_slow_non_contiguous=True),
                    g.dma_start(out=dbg["cols_" + tag][:, 3:4], in_=b["stdcol"][:], allow_slow_non_contiguous=True),
                    g.dma_start(out=dbg["cols_" + tag][:, 4:5], in_=b["invstd"][:], allow_slow_non_contiguous=True)]

    def ln(srcb, dstb, affine):
        @ctx.step("v", ctx.bar())
        def _(v):
            return v.tensor_reduce(b["SCRC1"][:], srcb[:], AX, ALU.add)

        @ctx.step("v", ctx.bar())
        def _(v):
            return v.tensor_scalar(b["negmu"][:], b["SCRC1"][:], -1.0 / D, None, ALU.mult)

        @ctx.step("s", ctx.bar())
        def _(s):
            return s.activation(b["EXP1"][:], srcb[:], ACT.Square)

        @ctx.step("v", ctx.bar())
        def _(v):
            return v.tensor_reduce(b["ssq"][:], b["EXP1"][:], AX, ALU.add)

        @ctx.step("v", ctx.bar())
        def _(v):
            v.tensor_scalar(b["var"][:], b["ssq"][:], 1.0 / D, None, ALU.mult)
            return v.tensor_tensor(b["SCRC1"][:], b["negmu"][:], b["negmu"][:], ALU.mult)

        @ctx.step("v", ctx.bar())
        def _(v):
            return v.tensor_tensor(b["var"][:], b["var"][:], b["SCRC1"][:], ALU.subtract)

        @ctx.step("s", ctx.bar())
        def _(s):
            return s.activation(b["stdcol"][:], b["var"][:], ACT.Sqrt, bias=b["epsc"][:, 0:1])

        @ctx.step("v", ctx.bar())
        def _(v):
            return v.reciprocal(b["invstd"][:], b["stdcol"][:])

        @ctx.step("v", ctx.bar())
        def _(v):
            last = v.tensor_scalar(dstb[:], srcb[:], b["negmu"][:], b["invstd"][:],
                                   ALU.add, ALU.mult)
            if affine:
                v.tensor_tensor(dstb[:], dstb[:], b["ONW_REP"][:], ALU.mult)
                last = v.tensor_tensor(dstb[:], dstb[:], b["ONB_REP"][:], ALU.add)
            return last

    ln(b["OD"], b["Rp"], True)
    ln(b["Rp"], b["X"], False)

    @ctx.step("v", ctx.bar())
    def _(v):
        return v.tensor_tensor(y1[:], xt[:], b["X"][:], ALU.add)

    @ctx.dma("g", ctx.bar())
    def _(g):
        return [g.dma_start(out=io["y1"][bb, cc * C:(cc + 1) * C, :], in_=y1[:])]


def build_attn(nc, B_LOC=2, L=2048, dbg_tiles=(), limit=10**9):
    NCH = L // C
    io = {
        "x": nc.dram_tensor("x", [B_LOC, L, D], F32, kind="ExternalInput").ap(),
        "W": nc.dram_tensor("W", [D, D], F32, kind="ExternalInput").ap(),
        "knw": nc.dram_tensor("knw", [1, 64], F32, kind="ExternalInput").ap(),
        "onw": nc.dram_tensor("onw", [1, D], F32, kind="ExternalInput").ap(),
        "onb": nc.dram_tensor("onb", [1, D], F32, kind="ExternalInput").ap(),
        "eta": nc.dram_tensor("eta", [1, H], F32, kind="ExternalInput").ap(),
        "lamlog": nc.dram_tensor("lamlog", [1, H], F32, kind="ExternalInput").ap(),
        "y1": nc.dram_tensor("y1", [B_LOC, L, D], F32, kind="ExternalOutput").ap(),
    }
    dbg = None
    if dbg_tiles:
        dbg = {"tiles": set(dbg_tiles)}
        for ti in dbg_tiles:
            tag = "t%d" % ti
            for nm, shape, dt in [("xhat", [128, D], F32), ("static", [128, D], F32),
                                  ("NR", [128, D], F32), ("NX", [128, D], F32),
                                  ("OD", [128, D], F32), ("S", [64, B_LOC * H * DH], F32),
                                  ("cols", [128, 8], F32), ("EXP1", [128, D], F32)]:
                dbg[nm + "_" + tag] = nc.dram_tensor(
                    "dbg_%s_%s" % (nm, tag), shape, dt, kind="ExternalOutput").ap()

    cst = inline_consts(nc)
    stack = contextlib.ExitStack()
    with stack:
        b = alloc_attn(nc, stack, B_LOC)
        with nc.Block() as block:
            ctx = Ctx(nc, block, stack, limit=limit)
            emit_attn_setup(ctx, b, io, cst, B_LOC)
            total = B_LOC * NCH

            def mkload(idx, deps):
                bb2, cc2 = divmod(idx, NCH)

                @ctx.dma("i", deps)
                def _(sy):
                    return [sy.dma_start(out=b["xt"][idx % 2][:],
                                         in_=io["x"][bb2, cc2 * C:(cc2 + 1) * C, :])]

            mkload(0, {})
            if total > 1:
                mkload(1, {})
            for ti in range(total):
                bb, cc = divmod(ti, NCH)
                emit_attn_tile(ctx, b, io, bb, cc, ti, B_LOC, dbg)
                if ti + 2 < total:
                    mkload(ti + 2, ctx.bar())

            ctx.wait("g", {"g": ctx.c["g"]})
    return io


"""Phase B builder: routing-ffn block, l-sharded, token-major tiles [8l x 16s, D]."""

LTOT = 2048
SCALE = 0.125


def ffn_consts(nc):
    hm = np.zeros((H, 64, 128), dtype=np.float32)
    for h in range(H):
        for l in range(8):
            hm[h, :, l * 16 + h] = 1.0
    inds = np.zeros((64, H * 16), dtype=np.float32)
    for h in range(H):
        inds[:, h * 16 + h] = 1.0
    return {
        "HMASKS": nc.inline_tensor(hm.reshape(H * 64, 128).astype(BF), name="f_hmasks").ap(),
        "INDS": nc.inline_tensor(inds.astype(BF), name="f_inds").ap(),
        "IDENTF": nc.inline_tensor(np.eye(128, dtype=np.float32), name="f_identf").ap(),
        "IDENTB": nc.inline_tensor(np.eye(128, dtype=np.float32).astype(BF), name="f_identb").ap(),
        "ONESF": nc.inline_tensor(np.ones((1, 128), np.float32), name="f_ones").ap(),
        "BLOCKMASK": nc.inline_tensor(
            np.kron(np.eye(8, dtype=np.float32), np.ones((16, 16), np.float32)).astype(BF),
            name="f_bmask").ap(),
    }


def alloc_ffn(nc, stack, LS, mode):
    b = {}

    def S(name, shape, dt=F32):
        b[name] = stack.enter_context(nc.sbuf_tensor("f_" + name, shape, dt))

    NT = LS * 16 // 128
    b["NT"] = NT
    S("WgS", [128, 8 * 16])
    S("WqS", [128, H * 8 * 64], BF16)
    S("WkS", [128, H * 8 * 64], BF16)
    S("WvS", [128, H * 8 * 64], BF16)
    S("WoS", [64, H * 1024], BF16)
    S("LDTMP", [128, 1024])
    S("bq_s", [16, 64]); S("bkv_s", [16, 128])
    S("bqT", [64, 16]); S("bkT", [64, 16]); S("bvT", [64, 16])
    S("bo_r", [1, D]); S("lng_r", [1, D]); S("lnb_r", [1, D])
    S("BO_REP", [128, D]); S("LNG_REP", [128, D]); S("LNB_REP", [128, D])
    S("HM2", [128, H * 128], BF16)
    S("INDS", [64, H * 16], BF16)
    S("IDENTF", [128, 128]); S("IDENTB", [128, 128], BF16)
    S("cONES", [1, 128])
    S("epsc", [128, 2])
    S("ytile", [128, D])
    S("z", [128, D]); S("zbf", [128, D], BF16)
    S("EXP1", [128, D])
    S("zF", [128, D])
    S("glog", [128, 16])
    S("exps", [128, 16]); S("score", [128, 16], BF16)
    S("scoreF", [128, 16])
    S("scoreT", [16, 128]); S("pmax", [16, 8])
    S("EQ", [16, 128])
    S("bdA", [128, 128], BF16); S("EQTS", [128, 16], BF16); S("BLOCKMASK", [128, 128], BF16)
    S("bdB", [128, 0 + 32 * 128], BF16)
    S("statsSave", [128, 2 * 32])
    S("hgF", [128, 8 * 1024], BF16)
    S("qbf", [64, H * 64], BF16); S("kbf", [64, H * 64], BF16)
    S("qkm", [64, H * 64], BF16)
    S("vbf", [64, 4 * H * 64], BF16)
    S("attS", [16, LS])
    S("mrow", [16, 1]); S("srow", [16, 1]); S("lsum", [16, 1]); S("prec", [16, 1])
    S("junk16", [16, LTOT])
    S("pfull", [16, LS])
    S("pT", [128, ((LS + 127) // 128) * 16])
    S("pINT", [128, 32])
    S("VSELALL", [64, H * 128], BF16); S("OUT2", [128, 1024])
    S("aoT", [128, D], BF16)
    S("negmu", [128, 1]); S("ssq", [128, 1]); S("var", [128, 1])
    S("invstd", [128, 1]); S("stdcol", [128, 1]); S("SCRC1", [128, 1])
    if mode == "full":
        S("attF_s", [16, LTOT]); S("attFloc_s", [16, LS])
    b["p_scr"] = nc.dram_tensor("f_pscr", [128, 16 * ((LS + 127) // 128)], F32, kind="Internal").ap()
    b["pb"] = [stack.enter_context(nc.psum_tensor("fpb%d" % i, [128, 512], F32))
               for i in range(6)]
    b["pt"] = stack.enter_context(nc.psum_tensor("fpt", [128, 1024], BF16))
    b["ptf"] = stack.enter_context(nc.psum_tensor("fptf", [128, 512], F32))
    return b


def emit_ffn_setup(ctx, b, io, cst, mode):
    pb = b["pb"]

    @ctx.dma("i", {})
    def _(sy):
        out = []
        out.append(sy.dma_start(out=b["IDENTF"][:], in_=cst["IDENTF"]))
        out.append(sy.dma_start(out=b["IDENTB"][:], in_=cst["IDENTB"]))
        out.append(sy.dma_start(out=b["INDS"][:], in_=cst["INDS"]))
        hm = cst["HMASKS"]
        for h in range(H):
            out.append(sy.dma_start(out=b["HM2"][0:64, h * 128:(h + 1) * 128],
                                    in_=bass.AP(hm.tensor, hm.offset + h * 64 * 128,
                                                [[128, 64], [1, 128]])))
        out.append(sy.dma_start(out=b["cONES"][:], in_=cst["ONESF"]))
        out.append(sy.dma_start(out=b["BLOCKMASK"][:], in_=cst["BLOCKMASK"]))
        out.append(sy.dma_start(out=b["bq_s"][:], in_=io["bq"]))
        out.append(sy.dma_start(out=b["bkv_s"][:], in_=io["bkv"]))
        out.append(sy.dma_start(out=b["bo_r"][:], in_=io["bo"]))
        out.append(sy.dma_start(out=b["lng_r"][:], in_=io["lng"]))
        out.append(sy.dma_start(out=b["lnb_r"][:], in_=io["lnb"]))
        if mode == "full":
            out.append(sy.dma_start(out=b["attF_s"][:], in_=io["attF"]))
            out.append(sy.dma_start(out=b["attFloc_s"][:], in_=io["attFloc"]))
        return out

    @ctx.step("g", {})
    def _(g):
        g.memset(b["epsc"][:, 0:1], LN_EPS)
        g.memset(b["bdA"][:], 0.0)
        return g.memset(b["vbf"][:], 0.0)

    @ctx.dma("i", ctx.bar())
    def _(sy):
        return [sy.dma_start(out=b["LDTMP"][:, 0:128],
                             in_=bass.AP(io["Wg"].tensor, io["Wg"].offset,
                                         [[16, 128], [2048, 8], [1, 16]]))]

    @ctx.step("s", ctx.bar())
    def _(s):
        return s.copy(b["WgS"][:], b["LDTMP"][:, 0:128])

    for h in range(H):
        @ctx.dma("i", ctx.bar())
        def _(sy, h=h):
            wq = io["Wq"]
            out = [sy.dma_start(out=b["LDTMP"][:, 0:512],
                                in_=bass.AP(wq.tensor, wq.offset + h * 1024 * 64,
                                            [[64, 128], [8192, 8], [1, 64]]))]
            wkv = io["Wkv"]
            out.append(sy.dma_start(out=b["LDTMP"][:, 512:1024],
                       in_=bass.AP(wkv.tensor, wkv.offset + h * 1024 * 128,
                                   [[128, 128], [16384, 8], [1, 64]])))
            out.append(sy.dma_start(out=b["ytile"][:, 0:512],
                       in_=bass.AP(wkv.tensor, wkv.offset + h * 1024 * 128 + 64,
                                   [[128, 128], [16384, 8], [1, 64]])))
            return out

        @ctx.step("s", ctx.bar())
        def _(s, h=h):
            s.copy(b["WqS"][:, h * 512:(h + 1) * 512], b["LDTMP"][:, 0:512])
            s.copy(b["WkS"][:, h * 512:(h + 1) * 512], b["LDTMP"][:, 512:1024])
            return s.copy(b["WvS"][:, h * 512:(h + 1) * 512], b["ytile"][:, 0:512])

    for h in range(H):
        @ctx.dma("i", ctx.bar())
        def _(sy, h=h):
            wo = io["Wo"]
            return [sy.dma_start(out=b["LDTMP"][0:64, 0:1024],
                                 in_=bass.AP(wo.tensor, wo.offset + h * 64 * 1024,
                                             [[1024, 64], [1, 1024]]))]

        @ctx.step("s", ctx.bar())
        def _(s, h=h):
            return s.copy(b["WoS"][0:64, h * 1024:(h + 1) * 1024], b["LDTMP"][0:64, 0:1024])

    @ctx.step("t", ctx.bar())
    def _(t):
        t.transpose(b["ptf"][0:64, 0:16], b["bq_s"][:], b["IDENTF"][0:16, 0:16])
        t.transpose(b["ptf"][0:64, 16:32], b["bkv_s"][:, 0:64], b["IDENTF"][0:16, 0:16])
        return t.transpose(b["ptf"][0:64, 32:48], b["bkv_s"][:, 64:128], b["IDENTF"][0:16, 0:16])

    @ctx.step("v", ctx.bar())
    def _(v):
        v.tensor_copy(b["bqT"][:], b["ptf"][0:64, 0:16])
        v.tensor_copy(b["bkT"][:], b["ptf"][0:64, 16:32])
        return v.tensor_copy(b["bvT"][:], b["ptf"][0:64, 32:48])

    @ctx.step("t", ctx.bar())
    def _(t):
        t.matmul(pb[0][:, :], b["cONES"][:], b["bo_r"][:1, 0:512], start=True, stop=True)
        t.matmul(pb[1][:, :], b["cONES"][:], b["bo_r"][:1, 512:1024], start=True, stop=True)
        t.matmul(pb[2][:, :], b["cONES"][:], b["lng_r"][:1, 0:512], start=True, stop=True)
        t.matmul(pb[3][:, :], b["cONES"][:], b["lng_r"][:1, 512:1024], start=True, stop=True)
        t.matmul(pb[4][:, :], b["cONES"][:], b["lnb_r"][:1, 0:512], start=True, stop=True)
        return t.matmul(pb[5][:, :], b["cONES"][:], b["lnb_r"][:1, 512:1024], start=True, stop=True)

    @ctx.step("v", ctx.bar())
    def _(v):
        v.tensor_copy(b["BO_REP"][:, 0:512], pb[0][:])
        v.tensor_copy(b["BO_REP"][:, 512:1024], pb[1][:])
        v.tensor_copy(b["LNG_REP"][:, 0:512], pb[2][:])
        v.tensor_copy(b["LNG_REP"][:, 512:1024], pb[3][:])
        v.tensor_copy(b["LNB_REP"][:, 0:512], pb[4][:])
        return v.tensor_copy(b["LNB_REP"][:, 512:1024], pb[5][:])




def emit_ffn_tile_b1(ctx, b, io, ti, LS, mode, l0_global):
    pb = b["pb"]
    need_in = ctx.c["i"]
    st_slot = ti % 8

    @ctx.dma("i", ctx.bar())
    def _(sy):
        ys = io["yslab"]
        return [sy.dma_start(out=b["ytile"][:],
                             in_=bass.AP(ys.tensor, ys.offset + (l0_global + ti * 8) * D,
                                         [[D, 8], [io["yslab_pitch"], 16], [1, D]]))]

    @ctx.step("v", ctx.bar())
    def _(v):
        return v.tensor_reduce(b["SCRC1"][:], b["ytile"][:], AX, ALU.add)

    @ctx.step("v", ctx.bar())
    def _(v):
        return v.tensor_scalar(b["negmu"][:], b["SCRC1"][:], -1.0 / D, None, ALU.mult)

    @ctx.step("s", ctx.bar())
    def _(s):
        return s.activation(b["EXP1"][:], b["ytile"][:], ACT.Square)

    @ctx.step("v", ctx.bar())
    def _(v):
        return v.tensor_reduce(b["ssq"][:], b["EXP1"][:], AX, ALU.add)

    @ctx.step("v", ctx.bar())
    def _(v):
        v.tensor_scalar(b["var"][:], b["ssq"][:], 1.0 / D, None, ALU.mult)
        return v.tensor_tensor(b["SCRC1"][:], b["negmu"][:], b["negmu"][:], ALU.mult)

    @ctx.step("v", ctx.bar())
    def _(v):
        return v.tensor_tensor(b["var"][:], b["var"][:], b["SCRC1"][:], ALU.subtract)

    @ctx.step("s", ctx.bar())
    def _(s):
        return s.activation(b["stdcol"][:], b["var"][:], ACT.Sqrt, bias=b["epsc"][:, 0:1])

    @ctx.step("v", ctx.bar())
    def _(v):
        return v.reciprocal(b["invstd"][:], b["stdcol"][:])

    @ctx.step("v", ctx.bar())
    def _(v):
        v.tensor_copy(b["statsSave"][:, 2 * ti:2 * ti + 1], b["negmu"][:])
        v.tensor_copy(b["statsSave"][:, 2 * ti + 1:2 * ti + 2], b["invstd"][:])
        return v.tensor_scalar(b["z"][:], b["ytile"][:], b["negmu"][:], b["invstd"][:],
                               ALU.add, ALU.mult)

    @ctx.step("s", ctx.bar())
    def _(s):
        return s.copy(b["zbf"][:], b["z"][:])

    for wave in range(2):
        @ctx.step("t", ctx.bar())
        def _(t, wave=wave):
            last = None
            for i in range(4):
                ii = wave * 4 + i
                last = t.transpose(b["ptf"][:, i * 128:(i + 1) * 128],
                                   b["z"][:, ii * 128:(ii + 1) * 128], b["IDENTF"][:])
            return last

        @ctx.step("s", ctx.bar())
        def _(s, wave=wave):
            last = None
            for i in range(4):
                ii = wave * 4 + i
                last = s.copy(b["zF"][:, ii * 128:(ii + 1) * 128],
                              b["ptf"][:, i * 128:(i + 1) * 128])
            return last

    @ctx.step("t", ctx.bar())
    def _(t):
        last = None
        for j in range(8):
            last = t.matmul(b["ptf"][:, 0:16], b["zF"][:, j * 128:(j + 1) * 128],
                            b["WgS"][:, j * 16:(j + 1) * 16],
                            start=(j == 0), stop=(j == 7))
        return last

    @ctx.step("v", ctx.bar())
    def _(v):
        v.tensor_copy(b["glog"][:], b["ptf"][:, 0:16])
        return v.tensor_reduce(b["SCRC1"][:], b["ptf"][:, 0:16], AX, ALU.max)

    @ctx.step("v", ctx.bar())
    def _(v):
        return v.tensor_scalar(b["negmu"][:], b["SCRC1"][:], -1.0, None, ALU.mult)

    @ctx.step("s", ctx.bar())
    def _(s):
        return s.activation(b["exps"][:], b["glog"][:], ACT.Exp, bias=b["negmu"][:])

    @ctx.step("v", ctx.bar())
    def _(v):
        return v.tensor_reduce(b["ssq"][:], b["exps"][:], AX, ALU.add)

    @ctx.step("s", ctx.bar())
    def _(s):
        return s.activation(b["var"][:], b["ssq"][:], ACT.Ln)

    @ctx.step("v", ctx.bar())
    def _(v):
        return v.tensor_tensor(b["stdcol"][:], b["negmu"][:], b["var"][:], ALU.subtract)

    @ctx.step("v", ctx.bar())
    def _(v):
        return v.tensor_scalar(b["scoreF"][:], b["glog"][:], b["stdcol"][:], None, ALU.add)

    @ctx.step("t", ctx.bar())
    def _(t):
        return t.transpose(b["ptf"][0:16, 128:256], b["scoreF"][:], b["IDENTF"][:])

    @ctx.step("v", ctx.bar())
    def _(v):
        return v.tensor_copy(b["scoreT"][:], b["ptf"][0:16, 128:256])

    @ctx.step("v", ctx.bar())
    def _(v):
        return v.tensor_reduce(b["pmax"][:], ap3(b["scoreT"][:], [[16, 8], [1, 16]]),
                               AX, ALU.max)

    @ctx.step("v", ctx.bar())
    def _(v):
        pm = b["pmax"][:]
        return v.tensor_tensor(ap3(b["EQ"][:], [[16, 8], [1, 16]]),
                               ap3(b["scoreT"][:], [[16, 8], [1, 16]]),
                               bass.AP(pm.tensor, pm.offset, [[8, 16], [1, 8], [0, 16]]),
                               ALU.is_equal)

    @ctx.step("t", ctx.bar())
    def _(t):
        return t.transpose(b["ptf"][:, 256:272], b["EQ"][:], b["IDENTF"][0:16, 0:16])

    @ctx.step("v", ctx.bar())
    def _(v):
        return v.tensor_copy(b["EQTS"][:], b["ptf"][:, 256:272])

    @ctx.step("v", ctx.bar())
    def _(v):
        eq = b["EQTS"][:]
        return v.tensor_tensor(b["bdA"][:], b["BLOCKMASK"][:],
                               bass.AP(eq.tensor, eq.offset, [[16, 128], [0, 8], [1, 16]]),
                               ALU.mult)

    @ctx.step("t", ctx.bar())
    def _(t):
        return t.transpose(b["pt"][:, 0:128], b["bdA"][:], b["IDENTB"][:])

    @ctx.step("s", ctx.bar())
    def _(s):
        return s.copy(b["bdB"][:, ti * 128:(ti + 1) * 128], b["pt"][:, 0:128])

    if mode == "stats" and ti == 0 and "dbg_glog" in io:
        @ctx.dma("g", ctx.bar())
        def _(g):
            return [g.dma_start(out=io["dbg_glog"], in_=b["glog"][:]),
                    g.dma_start(out=io["dbg_score"], in_=b["scoreF"][:]),
                    g.dma_start(out=io["dbg_EQ"], in_=b["EQ"][:]),
                    g.dma_start(out=io["dbg_z"], in_=b["z"][:])]
    if mode == "stats" and "dbg_EQall" in io:
        @ctx.dma("g", ctx.bar(), )
        def _(g, ti=ti):
            o = io["dbg_EQall"]
            return [g.dma_start(out=bass.AP(o.tensor, o.offset + ti * 16 * 128,
                                            [[128, 16], [1, 128]]), in_=b["EQ"][:])]

    @ctx.step("t", ctx.bar())
    def _(t):
        last = None
        for j in range(8):
            last = t.matmul(pb[j // 4][:, (j % 4) * 128:(j % 4) * 128 + 128],
                            b["zbf"][:, j * 128:(j + 1) * 128], b["bdA"][:],
                            start=True, stop=True)
        return last

    @ctx.step("s", ctx.bar())
    def _(s):
        last = None
        for j in range(8):
            last = s.copy(b["hgF"][:, j * 1024 + st_slot * 128:j * 1024 + (st_slot + 1) * 128],
                          pb[j // 4][:, (j % 4) * 128:(j % 4) * 128 + 128])
        return last


def emit_ffn_proj(ctx, b, st, mode):
    pb = b["pb"]

    def rhs_ap(j, h):
        rhs = b["hgF"][:]
        return bass.AP(rhs.tensor, rhs.offset + j * 1024 + h,
                       [[8192, 128], [128, 8], [16, 8]])

    def proj(W, pbase):
        @ctx.step("t", ctx.bar())
        def _(t):
            last = None
            for h in range(H):
                for j in range(8):
                    last = t.matmul(pb[pbase + h // 8][0:64, (h % 8) * 64:(h % 8) * 64 + 64],
                                    b[W][:, (h * 8 + j) * 64:(h * 8 + j + 1) * 64],
                                    rhs_ap(j, h), start=(j == 0), stop=(j == 7))
            return last

    if mode == "stats":
        proj("WqS", 2)

        @ctx.step("v", ctx.bar())
        def _(v):
            last = None
            for h in range(H):
                last = v.tensor_scalar(b["qbf"][0:64, h * 64:(h + 1) * 64],
                                       pb[2 + h // 8][0:64, (h % 8) * 64:(h % 8) * 64 + 64],
                                       b["bqT"][0:64, h:h + 1], None, ALU.add)
            return last

        proj("WkS", 2)

        @ctx.step("v", ctx.bar())
        def _(v):
            last = None
            for h in range(H):
                last = v.tensor_scalar(b["kbf"][0:64, h * 64:(h + 1) * 64],
                                       pb[2 + h // 8][0:64, (h % 8) * 64:(h % 8) * 64 + 64],
                                       b["bkT"][0:64, h:h + 1], None, ALU.add)
            return last

        @ctx.step("v", ctx.bar())
        def _(v):
            return v.tensor_tensor(b["qkm"][:], b["qbf"][:], b["kbf"][:], ALU.mult)

        @ctx.step("t", ctx.bar())
        def _(t):
            last = None
            for h in range(H):
                last = t.matmul(b["ptf"][0:16, 256:320],
                                b["INDS"][:, h * 16:(h + 1) * 16],
                                b["qkm"][0:64, h * 64:(h + 1) * 64],
                                start=(h == 0), stop=(h == 15))
            return last

        @ctx.step("v", ctx.bar())
        def _(v, st=st):
            return v.tensor_scalar(b["attS"][:, st * 64:(st + 1) * 64],
                                   b["ptf"][0:16, 256:320], SCALE, None, ALU.mult)
    else:
        proj("WvS", 2)

        @ctx.step("v", ctx.bar())
        def _(v, st=st):
            last = None
            for h in range(H):
                last = v.tensor_scalar(
                    b["vbf"][0:64, (st * H + h) * 64:(st * H + h) * 64 + 64],
                    pb[2 + h // 8][0:64, (h % 8) * 64:(h % 8) * 64 + 64],
                    b["bvT"][0:64, h:h + 1], None, ALU.add)
            return last


def emit_ffn_softmax(ctx, b, LS):
    @ctx.step("v", ctx.bar())
    def _(v):
        return v.tensor_reduce(b["mrow"][:], b["attF_s"][:], AX, ALU.max)

    @ctx.step("v", ctx.bar())
    def _(v):
        return v.tensor_scalar(b["srow"][:], b["mrow"][:], -1.0, None, ALU.mult)

    @ctx.step("s", ctx.bar())
    def _(s):
        return s.activation(b["junk16"][:], b["attF_s"][:], ACT.Exp, bias=b["srow"][:])

    @ctx.step("v", ctx.bar())
    def _(v):
        return v.tensor_reduce(b["lsum"][:], b["junk16"][:], AX, ALU.add)

    @ctx.step("v", ctx.bar())
    def _(v):
        return v.reciprocal(b["prec"][:], b["lsum"][:])

    @ctx.step("s", ctx.bar())
    def _(s):
        return s.activation(b["junk16"][:, 0:LS], b["attFloc_s"][:], ACT.Exp, bias=b["srow"][:])

    @ctx.step("v", ctx.bar())
    def _(v):
        return v.tensor_scalar(b["pfull"][:], b["junk16"][:, 0:LS], b["prec"][:], None, ALU.mult)

    nblk = (LS + 127) // 128

    @ctx.step("t", ctx.bar())
    def _(t):
        last = None
        for blk in range(nblk):
            last = t.transpose(b["ptf"][:, blk * 16:(blk + 1) * 16],
                               b["pfull"][0:16, blk * 128:(blk + 1) * 128], b["IDENTF"][0:16, 0:16])
        return last

    @ctx.step("v", ctx.bar())
    def _(v):
        last = None
        for blk in range(nblk):
            last = v.tensor_copy(b["pT"][:, blk * 16:(blk + 1) * 16],
                                 b["ptf"][:, blk * 16:(blk + 1) * 16])
        return last

    NT = b["NT"]
    pitch = 16 * ((LS + 127) // 128)

    @ctx.dma("g", ctx.bar())
    def _(g):
        return [g.dma_start(out=b["p_scr"], in_=b["pT"][:])]

    @ctx.dma("g", ctx.bar())
    def _(g):
        out = []
        scr = b["p_scr"]
        for i in range(NT):
            src_ = bass.AP(scr.tensor, scr.offset + (8 * i % 128) * pitch + (i // 16) * 16,
                           [[pitch, 8], [1, 16]])
            out.append(g.dma_start(out=b["pINT"][:, i:i + 1], in_=src_,
                                   allow_slow_non_contiguous=True))
        return out


def emit_ffn_tile_b2(ctx, b, io, ti, LS):
    pb = b["pb"]

    @ctx.dma("i", ctx.bar())
    def _(sy):
        ys = io["yslab"]
        return [sy.dma_start(out=b["ytile"][:],
                             in_=bass.AP(ys.tensor, ys.offset + (ti * 8) * D,
                                         [[D, 8], [LS * D, 16], [1, D]]))]

    @ctx.step("v", ctx.bar())
    def _(v):
        return v.tensor_scalar(b["z"][:], b["ytile"][:],
                               b["statsSave"][:, 2 * ti:2 * ti + 1],
                               b["statsSave"][:, 2 * ti + 1:2 * ti + 2],
                               ALU.add, ALU.mult)

    hs = ti // 8
    tl = ti % 8

    @ctx.step("v", ctx.bar())
    def _(v):
        last = None
        vb = b["vbf"][:]
        for h in range(H):
            off = (hs * H + h) * 64 + tl * 8
            last = v.tensor_tensor(
                b["VSELALL"][0:64, h * 128:(h + 1) * 128],
                b["HM2"][0:64, h * 128:(h + 1) * 128],
                bass.AP(vb.tensor, vb.offset + off, [[vb.ap[0][0], 64], [1, 8], [0, 16]]),
                ALU.mult)
        return last

    @ctx.step("t", ctx.bar())
    def _(t):
        last = None
        for h in range(H):
            for k in range(2):
                last = t.matmul(pb[k][:, :], b["VSELALL"][0:64, h * 128:(h + 1) * 128],
                                b["WoS"][0:64, h * 1024 + k * 512:h * 1024 + (k + 1) * 512],
                                start=(h == 0), stop=(h == 15))
        return last

    @ctx.step("v", ctx.bar())
    def _(v):
        v.scalar_tensor_tensor(b["aoT"][:, 0:512], pb[0][:], b["pINT"][:, ti:ti + 1],
                               b["BO_REP"][:, 0:512], ALU.mult, ALU.add)
        return v.scalar_tensor_tensor(b["aoT"][:, 512:1024], pb[1][:], b["pINT"][:, ti:ti + 1],
                                      b["BO_REP"][:, 512:1024], ALU.mult, ALU.add)

    @ctx.step("t", ctx.bar())
    def _(t):
        t.matmul(pb[2][:, :], b["bdB"][:, ti * 128:(ti + 1) * 128], b["aoT"][:, 0:512],
                 start=True, stop=True)
        return t.matmul(pb[3][:, :], b["bdB"][:, ti * 128:(ti + 1) * 128],
                        b["aoT"][:, 512:1024], start=True, stop=True)

    @ctx.step("v", ctx.bar())
    def _(v):
        v.tensor_tensor(b["OUT2"][:, 0:512], b["z"][:, 0:512], pb[2][:], ALU.add)
        return v.tensor_tensor(b["OUT2"][:, 512:1024], b["z"][:, 512:1024], pb[3][:], ALU.add)

    if "dbg_aoT" in io and ti == 0:
        @ctx.step("s", ctx.bar())
        def _(s):
            return s.copy(b["EXP1"][:], b["aoT"][:])

        @ctx.dma("g", ctx.bar())
        def _(g):
            return [g.dma_start(out=io["dbg_aoT"], in_=b["EXP1"][:]),
                    g.dma_start(out=io["dbg_y"], in_=b["OUT2"][:]),
                    g.dma_start(out=io["dbg_p"], in_=b["pINT"][:]),
                    g.dma_start(out=io["dbg_pfull"], in_=b["pfull"][:]),
                    g.dma_start(out=io["dbg_pT"], in_=b["pT"][:])]

    def lnchain(srcb, dstb, affine):
        @ctx.step("v", ctx.bar())
        def _(v):
            return v.tensor_reduce(b["SCRC1"][:], srcb[:], AX, ALU.add)

        @ctx.step("v", ctx.bar())
        def _(v):
            return v.tensor_scalar(b["negmu"][:], b["SCRC1"][:], -1.0 / D, None, ALU.mult)

        @ctx.step("s", ctx.bar())
        def _(s):
            return s.activation(b["EXP1"][:], srcb[:], ACT.Square)

        @ctx.step("v", ctx.bar())
        def _(v):
            return v.tensor_reduce(b["ssq"][:], b["EXP1"][:], AX, ALU.add)

        @ctx.step("v", ctx.bar())
        def _(v):
            v.tensor_scalar(b["var"][:], b["ssq"][:], 1.0 / D, None, ALU.mult)
            return v.tensor_tensor(b["SCRC1"][:], b["negmu"][:], b["negmu"][:], ALU.mult)

        @ctx.step("v", ctx.bar())
        def _(v):
            return v.tensor_tensor(b["var"][:], b["var"][:], b["SCRC1"][:], ALU.subtract)

        @ctx.step("s", ctx.bar())
        def _(s):
            return s.activation(b["stdcol"][:], b["var"][:], ACT.Sqrt, bias=b["epsc"][:, 0:1])

        @ctx.step("v", ctx.bar())
        def _(v):
            return v.reciprocal(b["invstd"][:], b["stdcol"][:])

        @ctx.step("v", ctx.bar())
        def _(v):
            last = v.tensor_scalar(dstb[:], srcb[:], b["negmu"][:], b["invstd"][:],
                                   ALU.add, ALU.mult)
            if affine:
                v.tensor_tensor(dstb[:], dstb[:], b["LNG_REP"][:], ALU.mult)
                last = v.tensor_tensor(dstb[:], dstb[:], b["LNB_REP"][:], ALU.add)
            return last

    lnchain(b["OUT2"], b["z"], True)
    lnchain(b["z"], b["OUT2"], False)

    @ctx.step("v", ctx.bar())
    def _(v):
        return v.tensor_tensor(b["OUT2"][:], b["OUT2"][:], b["ytile"][:], ALU.add)

    @ctx.dma("g", ctx.bar())
    def _(g):
        o = io["out"]
        return [g.dma_start(out=bass.AP(o.tensor, o.offset + (ti * 8) * D,
                                        [[D, 8], [LS * D, 16], [1, D]]),
                            in_=b["OUT2"][:])]


def build_ffn(nc, LS=256, mode="full"):
    io = {
        "yslab": nc.dram_tensor("yslab", [16, LS, D], F32, kind="ExternalInput").ap(),
        "Wg": nc.dram_tensor("Wg", [D, 16], F32, kind="ExternalInput").ap(),
        "Wq": nc.dram_tensor("Wq", [H, D, 64], F32, kind="ExternalInput").ap(),
        "bq": nc.dram_tensor("bq", [16, 64], F32, kind="ExternalInput").ap(),
        "Wkv": nc.dram_tensor("Wkv", [H, D, 128], F32, kind="ExternalInput").ap(),
        "bkv": nc.dram_tensor("bkv", [16, 128], F32, kind="ExternalInput").ap(),
        "Wo": nc.dram_tensor("Wo", [H, 64, D], F32, kind="ExternalInput").ap(),
        "bo": nc.dram_tensor("bo", [1, D], F32, kind="ExternalInput").ap(),
        "lng": nc.dram_tensor("lng", [1, D], F32, kind="ExternalInput").ap(),
        "lnb": nc.dram_tensor("lnb", [1, D], F32, kind="ExternalInput").ap(),
    }
    io["yslab_pitch"] = LS * D
    if mode == "stats":
        io["attS_out"] = nc.dram_tensor("attS_out", [16, LS], F32, kind="ExternalOutput").ap()
        io["dbg_glog"] = nc.dram_tensor("dbg_glog", [128, 16], F32, kind="ExternalOutput").ap()
        io["dbg_score"] = nc.dram_tensor("dbg_score", [128, 16], F32, kind="ExternalOutput").ap()
        io["dbg_EQ"] = nc.dram_tensor("dbg_EQ", [16, 128], F32, kind="ExternalOutput").ap()
        io["dbg_EQall"] = nc.dram_tensor("dbg_EQall", [32, 16, 128], F32, kind="ExternalOutput").ap()
        io["dbg_z"] = nc.dram_tensor("dbg_z", [128, 1024], F32, kind="ExternalOutput").ap()
    else:
        io["attF"] = nc.dram_tensor("attF", [16, LTOT], F32, kind="ExternalInput").ap()
        io["dbg_aoT"] = nc.dram_tensor("dbg_aoT", [128, D], F32, kind="ExternalOutput").ap()
        io["dbg_y"] = nc.dram_tensor("dbg_y", [128, D], F32, kind="ExternalOutput").ap()
        io["dbg_p"] = nc.dram_tensor("dbg_p", [128, 32], F32, kind="ExternalOutput").ap()
        io["dbg_pfull"] = nc.dram_tensor("dbg_pfull", [16, LS], F32, kind="ExternalOutput").ap()
        io["dbg_pT"] = nc.dram_tensor("dbg_pT", [128, 32], F32, kind="ExternalOutput").ap()
        io["attFloc"] = nc.dram_tensor("attFloc", [16, LS], F32, kind="ExternalInput").ap()
        io["out"] = nc.dram_tensor("out", [16, LS, D], F32, kind="ExternalOutput").ap()

    cst = ffn_consts(nc)
    stack = contextlib.ExitStack()
    with stack:
        b = alloc_ffn(nc, stack, LS, mode)
        with nc.Block() as block:
            ctx = Ctx(nc, block, stack)
            emit_ffn_setup(ctx, b, io, cst, mode)
            NT = b["NT"]
            for st in range(NT // 8):
                for sl in range(8):
                    ti = st * 8 + sl
                    emit_ffn_tile_b1(ctx, b, io, ti, LS, mode, 0)
                emit_ffn_proj(ctx, b, st, mode)
            if mode == "stats":
                @ctx.dma("g", ctx.bar())
                def _(g):
                    return [g.dma_start(out=io["attS_out"], in_=b["attS"][:])]
            else:
                emit_ffn_softmax(ctx, b, LS)
                for ti in range(NT):
                    emit_ffn_tile_b2(ctx, b, io, ti, LS)
            ctx.wait("g", {"g": ctx.c["g"]})
    return io


from concourse.bass_utils import run_bass_kernel_spmd

_CACHE = {}


def _get_attn_nc():
    if "attn" not in _CACHE:
        nc = bass.Bass("TRN2", target_bir_lowering=False, debug=False)
        build_attn(nc, B_LOC=2, L=2048)
        _CACHE["attn"] = nc
    return _CACHE["attn"]


def _get_ffn_nc(mode):
    key = "ffn_" + mode
    if key not in _CACHE:
        nc = bass.Bass("TRN2", target_bir_lowering=False, debug=False)
        build_ffn(nc, LS=256, mode=mode)
        _CACHE[key] = nc
    return _CACHE[key]


def kernel(L0_attn_x, L0_attn_W, L0_attn_k_norm_w, L0_attn_o_norm_w, L0_attn_o_norm_b,
           L0_attn_eta_raw, L0_attn_lam_logit, L0_ffn_attn_mask, L0_ffn_Wg, L0_ffn_Wq,
           L0_ffn_bq, L0_ffn_Wkv, L0_ffn_bkv, L0_ffn_Wo, L0_ffn_bo, L0_ffn_ln_g,
           L0_ffn_ln_b):
    f32 = np.float32
    x = np.ascontiguousarray(L0_attn_x, dtype=f32)
    attn_common = {
        "W": np.ascontiguousarray(L0_attn_W, dtype=f32),
        "knw": np.ascontiguousarray(L0_attn_k_norm_w, dtype=f32).reshape(1, 64),
        "onw": np.ascontiguousarray(L0_attn_o_norm_w, dtype=f32).reshape(1, D),
        "onb": np.ascontiguousarray(L0_attn_o_norm_b, dtype=f32).reshape(1, D),
        "eta": np.ascontiguousarray(L0_attn_eta_raw, dtype=f32).reshape(1, H),
        "lamlog": np.ascontiguousarray(L0_attn_lam_logit, dtype=f32).reshape(1, H),
    }
    in_maps = [dict(attn_common, x=np.ascontiguousarray(x[2 * r:2 * r + 2]))
               for r in range(8)]
    res = run_bass_kernel_spmd(_get_attn_nc(), in_maps, core_ids=list(range(8)))
    y1 = np.concatenate([np.asarray(res.results[r]["y1"]) for r in range(8)], axis=0)

    ffn_common = {
        "Wg": np.ascontiguousarray(L0_ffn_Wg, dtype=f32),
        "Wq": np.ascontiguousarray(L0_ffn_Wq, dtype=f32),
        "bq": np.ascontiguousarray(L0_ffn_bq, dtype=f32),
        "Wkv": np.ascontiguousarray(L0_ffn_Wkv, dtype=f32),
        "bkv": np.ascontiguousarray(L0_ffn_bkv, dtype=f32),
        "Wo": np.ascontiguousarray(L0_ffn_Wo, dtype=f32),
        "bo": np.ascontiguousarray(L0_ffn_bo, dtype=f32).reshape(1, D),
        "lng": np.ascontiguousarray(L0_ffn_ln_g, dtype=f32).reshape(1, D),
        "lnb": np.ascontiguousarray(L0_ffn_ln_b, dtype=f32).reshape(1, D),
    }
    in_maps2 = [dict(ffn_common, yslab=np.ascontiguousarray(y1[:, 256 * r:256 * r + 256]))
                for r in range(8)]
    res2 = run_bass_kernel_spmd(_get_ffn_nc("stats"), in_maps2, core_ids=list(range(8)))
    attF = np.concatenate([np.asarray(res2.results[r]["attS_out"]) for r in range(8)], axis=1)
    attF = np.ascontiguousarray(attF)

    in_maps3 = [dict(ffn_common,
                     yslab=np.ascontiguousarray(y1[:, 256 * r:256 * r + 256]),
                     attF=attF,
                     attFloc=np.ascontiguousarray(attF[:, 256 * r:256 * r + 256]))
                for r in range(8)]
    res3 = run_bass_kernel_spmd(_get_ffn_nc("full"), in_maps3, core_ids=list(range(8)))
    out = np.concatenate([np.asarray(res3.results[r]["out"]) for r in range(8)], axis=1)
    return np.ascontiguousarray(out, dtype=f32)
